# version 01 (final)
# baseline (speedup 1.0000x reference)
import numpy as np
import ml_dtypes
import os
os.environ.setdefault("TRN_ALLOW_ICI_ALL_TO_ALL", "1")
import concourse.bass as bass
import concourse.mybir as mybir
from concourse.bass_utils import run_bass_kernel_spmd

F32 = mybir.dt.float32
BF16 = mybir.dt.bfloat16
AF = mybir.ActivationFunctionType
ALU = mybir.AluOpType
AX = mybir.AxisListType.X
BF = ml_dtypes.bfloat16

D, S, H, HD, FF, NE = 2048, 8192, 16, 128, 5632, 8
NCORE, TOK, KD, FC, NQT = 8, 1024, 16, 44, 16
INVSQ = float(1.0 / np.sqrt(128.0))
EXPB = -8.0
LSC = 2.0

_TAPS = {}
_PHLIM = 99
_P6LIM = 99

A_OFF = 0
B_OFF = 90112
C_OFF = 122880
D_OFF = 155648
E_OFF = 193536
ARENA = 212832


def _owner(b):
    return b if b < 8 else 15 - b


def _loff(b):
    return 0 if b < 8 else 512


class Ctx:

    def __init__(self, nc, cur, engines, sems):
        self.nc, self.cur, self.E, self.S = nc, cur, engines, sems
        self.c = {k: 0 for k in sems}
        self.last_pe = None

    def on(self, name):
        return self.cur == name

    def inc(self, sem, inst, k=1):
        self.c[sem] += k
        if inst is not None:
            inst.then_inc(self.S[sem], k)
        return self.c[sem]

    def pemark(self):
        inst = self.last_pe if self.cur == 't' else None
        assert not (self.cur == 't') or inst is not None
        self.last_pe = None
        return self.inc('pe', inst)

    def n(self, sem):
        return self.c[sem]

    def wait(self, eng, sem, val):
        if self.cur == eng and val > 0:
            self.E[eng].wait_ge(self.S[sem], val)


def DMA(ctx, q, out, in_, sem=None):
    sem = sem or ('ld' if q == 'g' else 'sy')
    inst = ctx.E[q].dma_start(out=out, in_=in_) if ctx.on(q) else None
    return ctx.inc(sem, inst, 16)


def MM(ctx, out, l, r, start, stop, tp=None):
    if ctx.on('t'):
        kw = {} if tp is None else dict(tile_position=tp)
        ctx.last_pe = ctx.E['t'].matmul(out, l, r, start=start, stop=stop, **kw)


def OP(ctx, eng, fname, *a, sem=None, **kw):
    inst = getattr(ctx.E[eng], fname)(*a, **kw) if ctx.on(eng) else None
    if inst is not None and eng == 't':
        ctx.last_pe = inst
    if sem is not None:
        return ctx.inc(sem, inst)
    return None


def BAR(ctx):
    marks = [(sm, ctx.n(sm)) for sm in ('pe', 've', 'sc', 'ld', 'sy')]
    for eng in ('t', 'v', 's', 'g', 'y'):
        for sm, val in marks:
            ctx.wait(eng, sm, val)


def _emit(ctx, T, P, DT):
    def pb(b, w=1, rows=128):
        return P[0:rows, 512 * b:512 * b + 512 * w]

    for nm in ('consts', 'identb', 'identf', 'onesrow_f', 'onesrow_b',
               'onescol_b', 'invcol_b', 'bq128', 'bo128', 'gts', 'cmat',
               'bvrow', 'zrow'):
        DMA(ctx, 'g', T[nm][:, :], DT[nm])
    ld_consts = ctx.n('ld')
    ld_x = DMA(ctx, 'y', T['xf'][:, :].rearrange("p (k t) -> p k t", k=KD),
               DT['x'].rearrange("(k p) t -> p k t", p=128), sem='sy')
    if _PHLIM <= -6:
        ctx.wait('y', 'sy', ld_x)
        ctx.wait('g', 'ld', ctx.n('ld'))
        return

    ctx.wait('t', 'ld', ld_consts)
    OP(ctx, 't', 'transpose', pb(7)[0:128, 0:2], T['cmat'][:, :],
       T['identf'][0:2, 0:2])
    pe_cm = ctx.pemark()
    ctx.wait('s', 'pe', pe_cm)
    sc_sil = OP(ctx, 's', 'activation', T['siluc_f'][:, :], pb(7)[0:128, 0:2],
                AF.Silu, sem='sc')
    ve_ev = 0
    sy_mod = 0
    for grp in range(12):
        wm = T['wmslab'][grp % 2]
        if grp >= 2:
            ctx.wait('y', 'pe', ctx.n('pe'))
        ld_wm = DMA(ctx, 'y', wm[:, :].rearrange("p (k m) -> p k m", k=2),
                    DT['wmod_s'].rearrange("(k p) m -> p k m", p=128)
                    [:, :, 1024 * grp:1024 * (grp + 1)], sem='sy')
        ctx.wait('t', 'sy', ld_wm)
        ctx.wait('t', 'sc', sc_sil)
        for sub in range(2):
            bank = 6 + sub
            ctx.wait('t', 've', ve_ev)
            for k in range(2):
                MM(ctx, pb(bank, rows=1),
                   T['siluc_f'][:, k:k + 1],
                   wm[:, 1024 * k + 512 * sub:1024 * k + 512 * sub + 512],
                   start=(k == 0), stop=(k == 1))
            pe_m = ctx.pemark()
            ctx.wait('v', 'pe', pe_m)
            ve_ev = OP(ctx, 'v', 'tensor_copy', T['modstage'][:, :],
                       pb(bank, rows=1), sem='ve')
            ctx.wait('y', 've', ve_ev)
            idx = 2 * grp + sub
            mi24 = DT['mod_in'].rearrange("(a b) p -> a (b p)", b=4)
            sy_mod = DMA(ctx, 'y', mi24[idx:idx + 1, :], T['modstage'][:, :],
                         sem='sy')
    pe_mod = ctx.n('pe')
    if ctx.on('g'):
        ctx.E['g'].wait_ge(ctx.S['sy'], sy_mod)
        ctx.E['g'].collective_compute(
            "AllReduce", ALU.add, replica_groups=[list(range(NCORE))],
            ins=[DT['mod_in']], outs=[DT['mod_out']]).then_inc(ctx.S['c1'])
    ctx.c['c1'] += 1
    ctx.wait('g', 'c1', 1)
    ld_mo = DMA(ctx, 'g', T['modin96'][:, :], DT['mod_out'])
    ctx.wait('t', 'ld', ld_mo)
    OP(ctx, 't', 'transpose', pb(6)[0:128, 0:96], T['modin96'][:, :],
       T['identf'][0:96, 0:96])
    pe_mt = ctx.pemark()
    ctx.wait('v', 'pe', pe_mt)
    ld_modr = ctx.n('ld')
    ve_modr = OP(ctx, 'v', 'tensor_copy', T['modr'][:, :], pb(6)[0:128, 0:96],
                 sem='ve')

    ctx.wait('v', 'sy', ld_x)
    ctx.wait('s', 'sy', ld_x)
    pe_statk = [0] * KD
    for k in range(KD):
        if k >= 2:
            ctx.wait('v', 'pe', pe_statk[k - 2])
            ctx.wait('s', 'pe', pe_statk[k - 2])
        ve_xb = OP(ctx, 'v', 'tensor_copy', T['xbk'][k % 2][:, :],
                   T['xf'][:, 1024 * k:1024 * (k + 1)], sem='ve')
        sc_sq = OP(ctx, 's', 'activation', T['sqk'][k % 2][:, :],
                   T['xf'][:, 1024 * k:1024 * (k + 1)], AF.Square, sem='sc')
        ctx.wait('t', 've', ve_xb)
        ctx.wait('t', 'sc', sc_sq)
        for hf in range(2):
            MM(ctx, pb(0, 2)[0:1, 512 * hf:512 * hf + 512], T['invcol_b'][:, :],
               T['xbk'][k % 2][:, 512 * hf:512 * hf + 512],
               start=(k == 0), stop=(k == KD - 1))
            MM(ctx, pb(2, 2)[0:1, 512 * hf:512 * hf + 512], T['invcol_b'][:, :],
               T['sqk'][k % 2][:, 512 * hf:512 * hf + 512],
               start=(k == 0), stop=(k == KD - 1))
        pe_statk[k] = ctx.pemark()
    ctx.wait('y', 'out', ctx.n('out'))
    if _PHLIM <= -2:
        ctx.wait('v', 'pe', ctx.n('pe'))
        return

    def _ln_finalize():
        ctx.wait('v', 'pe', ctx.n('pe'))
        ve_m = OP(ctx, 'v', 'tensor_copy', T['mrow'][:, :], pb(0, 2)[0:1, :],
                  sem='ve')
        ctx.wait('s', 've', ve_m)
        sc_m2 = OP(ctx, 's', 'activation', T['rowscr'][:, :], T['mrow'][:, :],
                   AF.Square, sem='sc')
        ctx.wait('v', 'sc', sc_m2)
        ve_var = OP(ctx, 'v', 'tensor_sub', T['rowscr'][:, :], pb(2, 2)[0:1, :],
                    T['rowscr'][:, :], sem='ve')
        ctx.wait('s', 've', ve_var)
        OP(ctx, 's', 'activation', T['rowscr'][:, :], T['rowscr'][:, :], AF.Ln,
           bias=T['consts'][0:1, 1:2], sem='sc')
        ctx.wait('s', 'sc', ctx.n('sc'))
        sc_rstd = OP(ctx, 's', 'activation', T['rowscr'][:, :], T['rowscr'][:, :],
                     AF.Exp, scale=-0.5, sem='sc')
        ctx.wait('t', 'sc', sc_rstd)
        for hf in range(2):
            MM(ctx, pb(0, 2)[:, 512 * hf:512 * hf + 512], T['onesrow_f'][:, :],
               T['mrow'][0:1, 512 * hf:512 * hf + 512], start=True, stop=True)
            MM(ctx, pb(2, 2)[:, 512 * hf:512 * hf + 512], T['onesrow_f'][:, :],
               T['rowscr'][0:1, 512 * hf:512 * hf + 512], start=True, stop=True)
        pe_bc = ctx.pemark()
        ctx.wait('v', 'pe', pe_bc)
        OP(ctx, 'v', 'tensor_copy', T['mb'][:, :], pb(0, 2), sem='ve')
        return OP(ctx, 'v', 'tensor_copy', T['rb'][:, :], pb(2, 2), sem='ve')

    ve_bc = _ln_finalize()

    ctx.wait('v', 've', ve_modr)
    OP(ctx, 'v', 'tensor_scalar', T['modp1'][:, :], T['modr'][:, :], 1.0, None,
       ALU.add, sem='ve')
    ctx.wait('s', 've', ve_modr)
    for gnm, mbase in (('ga', 32), ('gb', 80)):
        sc_abs = OP(ctx, 's', 'activation', T[gnm][:, :],
                    T['modr'][:, mbase:mbase + 16], AF.Abs, sem='sc')
        ctx.wait('v', 'sc', sc_abs)
        OP(ctx, 'v', 'tensor_scalar', T[gnm][:, :], T[gnm][:, :],
           0.25, None, ALU.max, sem='ve')
        OP(ctx, 'v', 'tensor_scalar', T['gsn'][:, :], T['modr'][:, mbase:mbase + 16],
           0.0, None, ALU.is_ge, sem='ve')
        ctx.wait('v', 've', ctx.n('ve'))
        OP(ctx, 'v', 'tensor_scalar', T['gsn'][:, :], T['gsn'][:, :],
           2.0, -1.0, ALU.mult, ALU.add, sem='ve')
        ctx.wait('v', 've', ctx.n('ve'))
        OP(ctx, 'v', 'tensor_mul', T[gnm][:, :], T[gnm][:, :], T['gsn'][:, :],
           sem='ve')
    ctx.wait('v', 've', ctx.n('ve'))
    OP(ctx, 'v', 'tensor_mul', T['gabout'][:, :], T['ga'][:, :], T['bo128'][:, :],
       sem='ve')

    ctx.wait('v', 've', ve_bc)
    for k in range(KD):
        OP(ctx, 'v', 'tensor_sub', T['tscr'][:, :],
           T['xf'][:, 1024 * k:1024 * (k + 1)], T['mb'][:, :])
        OP(ctx, 'v', 'tensor_mul', T['tscr'][:, :], T['tscr'][:, :], T['rb'][:, :])
        ctx.wait('v', 've', ctx.n('ve'))
        OP(ctx, 'v', 'tensor_scalar', T['hh'][:, 1024 * k:1024 * (k + 1)],
           T['tscr'][:, :], T['modp1'][:, 16 + k:17 + k],
           T['modr'][:, 0 + k:1 + k], ALU.mult, ALU.add, sem='ve')
    ve_h = ctx.n('ve')
    ctx.wait('y', 've', ve_h)
    sy_xspill = ctx.n('sy')
    if 'h' in _TAPS:
        DMA(ctx, 'y', DT['tap_h'].rearrange("(k p) t -> p k t", p=128),
            T['hh'][:, :].rearrange("p (k t) -> p k t", k=KD), sem='out')

    ctx.wait('y', 'out', ctx.n('out'))
    if _PHLIM <= 0:
        return
    ctx.wait('g', 'pe', pe_mod)
    ctx.wait('g', 've', ve_h)
    pe_slab_done = [0] * 12
    sy_send = {}
    for sidx in range(12):
        wq = T['wslab'][sidx % 2]
        if sidx >= 2:
            ctx.wait('g', 'pe', pe_slab_done[sidx - 2])
        ld_s = DMA(ctx, 'g', wq[:, :].rearrange("p (k e) -> p k e", k=KD),
                   DT['wqkvT'].rearrange("(k p) e -> p k e", p=128)
                   [:, :, 512 * sidx:512 * (sidx + 1)])
        ctx.wait('t', 'ld', ld_s)
        ctx.wait('t', 've', ve_h)
        if sidx < 8:
            for fc in range(4):
                fglob = 4 * sidx + fc
                bank = (fc % 2) * 2
                ctx.wait('t', 'sc', ctx.n('sc'))
                if sidx == 4 and fc == 0:
                    ctx.wait('s', 'sy', sy_send['q'])
                for k in range(KD):
                    for hf in range(2):
                        MM(ctx, pb(bank, 2)[:, 512 * hf:512 * hf + 512],
                           wq[:, 512 * k + 128 * fc:512 * k + 128 * fc + 128],
                           T['hh'][:, 1024 * k + 512 * hf:1024 * k + 512 * hf + 512],
                           start=(k == 0), stop=(k == KD - 1))
                pe_mm = ctx.pemark()
                ctx.wait('s', 'pe', pe_mm)
                OP(ctx, 's', 'activation',
                   T['qksend'][:, 1024 * (fglob % 16):1024 * (fglob % 16) + 1024],
                   pb(bank, 2), AF.Identity, bias=T['bq128'][:, fglob:fglob + 1],
                   sem='sc')
            if sidx % 4 == 3:
                key = 'q' if sidx == 3 else 'k'
                ctx.wait('y', 'sc', ctx.n('sc'))
                sy_send[key] = DMA(
                    ctx, 'y', DT['ci_' + key].rearrange("(f p) t -> p f t", p=128),
                    T['qksend'][:, :].rearrange("p (f t) -> p f t", f=16), sem='sy')
        else:
            sv = sidx - 8
            for tt in range(8):
                bank = 4 + (tt % 2)
                ctx.wait('t', 'sc', ctx.n('sc'))
                for k in range(KD):
                    MM(ctx, pb(bank),
                       T['hh'][:, 1024 * k + 128 * tt:1024 * k + 128 * tt + 128],
                       wq[:, 512 * k:512 * k + 512], start=(k == 0), stop=False)
                MM(ctx, pb(bank), T['onesrow_b'][0:1, 0:128],
                   T['bvrow'][0:1, 512 * sv:512 * sv + 512], start=False, stop=True)
                pe_mm = ctx.pemark()
                ctx.wait('s', 'pe', pe_mm)
                ctx.wait('s', 'sy', ctx.n('sy') if (tt >= 2 or sidx > 8) else 0)
                sc_ev = OP(ctx, 's', 'activation', T['vstage'][tt % 2][:, :],
                           pb(bank), AF.Copy, sem='sc')
                ctx.wait('y', 'sc', sc_ev)
                dst = DT["ci_v"].rearrange("(j q p) f -> j q p f", j=8, p=128)
                sy_send['v'] = DMA(
                    ctx, 'y', dst[2 * sv:2 * sv + 2, tt, :, :].transpose([1, 0, 2]),
                    T['vstage'][tt % 2][:, :].rearrange("p (j f) -> p j f", j=2),
                    sem='sy')
        pe_slab_done[sidx] = ctx.n('pe')
    for key, csem in (('q', 'c2'), ('k', 'c3'), ('v', 'c4')):
        if ctx.on('g'):
            ctx.E['g'].wait_ge(ctx.S['sy'], sy_send[key])
            ctx.E['g'].collective_compute(
                "AllToAll", ALU.bypass, replica_groups=[list(range(NCORE))],
                ins=[DT['ci_' + key]], outs=[DT['co_' + key]]).then_inc(ctx.S[csem])
        ctx.c[csem] += 1
    if ctx.on('g'):
        ctx.E['g'].collective_compute(
            "AllToAll", ALU.bypass, replica_groups=[list(range(NCORE))],
            ins=[DT['bar_in']], outs=[DT['bar_out']]).then_inc(ctx.S['c2'])
    ctx.c['c2'] += 1

    ctx.wait('y', 'sy', sy_xspill)
    ctx.wait('y', 'c2', 2)
    for b in range(16):
        j, lo = _owner(b), _loff(b)
        for lh in range(2):
            DMA(ctx, 'y', T['Qm'][:, 8192 * lh + 512 * b:8192 * lh + 512 * b + 512],
                DT['co_q'][256 * j + 128 * lh:256 * j + 128 * lh + 128, lo:lo + 512],
                sem='sy')
    for b in range(16):
        j, lo = _owner(b), _loff(b)
        for lh in range(2):
            DMA(ctx, 'y', T['Kres'][:, 8192 * lh + 512 * b:8192 * lh + 512 * b + 512],
                DT['co_k'][256 * j + 128 * lh:256 * j + 128 * lh + 128, lo:lo + 512],
                sem='sy')
    for b in range(16):
        j, lo = _owner(b), _loff(b)
        for lh in range(2):
            dstv = T['Vres'][:, :].rearrange("p (kt x f) -> p kt x f", kt=64, x=2)
            src = DT['co_v'][1024 * j + lo:1024 * j + lo + 512,
                             128 * lh:128 * lh + 128]
            DMA(ctx, 'y', dstv[:, 4 * b:4 * b + 4, lh, :],
                src.rearrange("(s p) f -> p s f", p=128), sem='sy')
    sy_kvq = ctx.n('sy')
    if 'qkvres' in _TAPS:
        ctx.wait('y', 'sy', sy_kvq)
        DMA(ctx, 'y', DT['tap_qm'], T['Qm'][:, :], sem='out')
        DMA(ctx, 'y', DT['tap_kres'], T['Kres'][:, :], sem='out')
        DMA(ctx, 'y', DT['tap_vres'], T['Vres'][:, :], sem='out')
    if 'qsend' in _TAPS:
        DMA(ctx, 'y', DT['tap_qsend'], DT['co_q'], sem='out')
        DMA(ctx, 'y', DT['tap_vsend'], DT['co_v'], sem='out')
    DMA(ctx, 'g', T['masks'][:, :], DT['masks'])
    ld_masks = ctx.n('ld')
    BAR(ctx)

    ctx.wait('y', 'out', ctx.n('out'))
    if _PHLIM <= 1:
        return
    ld_wo = [0] * 8
    for ws in range(2):
        ld_wo[ws] = DMA(ctx, 'g', T['woslab'][ws % 2][:, :].rearrange(
            "p (k e) -> p k e", k=KD),
            DT['woutT'].rearrange("(k p) e -> p k e", p=128)
            [:, :, 256 * ws:256 * (ws + 1)])
    ctx.wait('t', 'sy', sy_kvq)
    ctx.wait('t', 'ld', ld_masks)
    last_exp_bank = [0, 0]
    last_av_bank = [0, 0]
    ve_norm = {}
    ve_rec_prev = 0
    for lh in range(2):
        for qt in range(NQT):
            nkt = 4 * qt + 4
            ob = 2 + (qt % 2)
            lb = 4 + (qt % 2)
            key_prev = (lh, qt - 2) if qt >= 2 else (
                (lh - 1, NQT - 2 + qt) if lh else None)
            if key_prev in ve_norm:
                ctx.wait('t', 've', ve_norm[key_prev])
            for kt in range(nkt):
                sb = kt % 2
                ctx.wait('t', 'sc', last_exp_bank[sb])
                MM(ctx, pb(sb),
                   T['Kres'][:, 8192 * lh + 128 * kt:8192 * lh + 128 * kt + 128],
                   T['Qm'][:, 8192 * lh + 512 * qt:8192 * lh + 512 * qt + 512],
                   start=True, stop=True)
                pe_s = ctx.pemark()
                ctx.wait('s', 'pe', pe_s)
                ctx.wait('s', 'pe', last_av_bank[sb])
                sc_e = OP(ctx, 's', 'activation', T['probs'][sb][:, :], pb(sb),
                          AF.Exp, bias=T['consts'][:, 0:1], scale=INVSQ, sem='sc')
                last_exp_bank[sb] = sc_e
                diag = kt - 4 * qt
                if 0 <= diag <= 3:
                    ctx.wait('v', 'sc', sc_e)
                    ve_m = OP(ctx, 'v', 'tensor_mul', T['probs'][sb][:, :],
                              T['probs'][sb][:, :],
                              T['masks'][:, 512 * diag:512 * diag + 512], sem='ve')
                    ctx.wait('t', 've', ve_m)
                else:
                    ctx.wait('t', 'sc', sc_e)
                MM(ctx, pb(ob),
                   T['Vres'][:, 256 * kt + 128 * lh:256 * kt + 128 * lh + 128],
                   T['probs'][sb][:, :], start=(kt == 0), stop=(kt == nkt - 1))
                MM(ctx, pb(lb, rows=1), T['onescol_b'][:, :], T['probs'][sb][:, :],
                   start=(kt == 0), stop=(kt == nkt - 1))
                last_av_bank[sb] = ctx.pemark()
            pe_last = last_av_bank[(nkt - 1) % 2]
            ctx.wait('s', 'pe', pe_last)
            OP(ctx, 's', 'activation', T['lnl'][:, :], pb(lb, rows=1), AF.Ln,
               sem='sc')
            ctx.wait('s', 'sc', ctx.n('sc'))
            sc_rec = OP(ctx, 's', 'activation', T['recrow'][:, :], T['lnl'][:, :],
                        AF.Exp, scale=-1.0, sem='sc')
            ctx.wait('t', 'sc', sc_rec)
            ctx.wait('t', 've', ve_rec_prev)
            MM(ctx, pb(6), T['onesrow_f'][:, :], T['recrow'][0:1, :],
               start=True, stop=True)
            pe_rb = ctx.pemark()
            ctx.wait('v', 'pe', pe_rb)
            ve_rec_prev = OP(ctx, 'v', 'tensor_copy', T['recsb'][:, :], pb(6),
                             sem='ve')
            ctx.wait('v', 've', ve_rec_prev)
            ve_norm[(lh, qt)] = OP(
                ctx, 'v', 'tensor_mul',
                T['hh'][:, 8192 * lh + 512 * qt:8192 * lh + 512 * qt + 512],
                pb(ob), T['recsb'][:, :], sem='ve')
    ve_attres = ctx.n('ve')
    ctx.wait('y', 've', ve_attres)
    for j in range(NCORE):
        for lh in range(2):
            for half, b in ((0, j), (1, 15 - j)):
                DMA(ctx, 'y',
                    DT['ci_a'][256 * j + 128 * lh:256 * j + 128 * lh + 128,
                               512 * half:512 * half + 512],
                    T['hh'][:, 8192 * lh + 512 * b:8192 * lh + 512 * b + 512],
                    sem='sy')
    sy_asend = ctx.n('sy')
    if ctx.on('g'):
        ctx.E['g'].wait_ge(ctx.S['sy'], sy_asend)
        ctx.E['g'].collective_compute(
            "AllToAll", ALU.bypass, replica_groups=[list(range(NCORE))],
            ins=[DT['ci_a']], outs=[DT['co_a']]).then_inc(ctx.S['c5'])
        ctx.E['g'].collective_compute(
            "AllToAll", ALU.bypass, replica_groups=[list(range(NCORE))],
            ins=[DT['bar_in']], outs=[DT['bar_out']]).then_inc(ctx.S['c5'])
    ctx.c['c5'] += 2
    ctx.wait('y', 'c5', 2)
    sy_abuf = DMA(ctx, 'y', T['abuf'][:, :].rearrange("p (k t) -> p k t", k=KD),
                  DT['co_a'].rearrange("(k p) t -> p k t", p=128), sem='sy')
    if 'attres' in _TAPS:
        DMA(ctx, 'y', DT['tap_attres'], DT['co_a'], sem='out')
    BAR(ctx)

    ctx.wait('y', 'out', ctx.n('out'))
    if _PHLIM <= 2:
        return
    sy_xc = [0] * KD
    sy_xc[0] = DMA(ctx, 'y', T['xchunk'][0][:, :], DT['x'][0:128, :], sem='sy')
    sy_xc[1] = DMA(ctx, 'y', T['xchunk'][1][:, :], DT['x'][128:256, :], sem='sy')
    if 'wo01' in _TAPS:
        ctx.wait('y', 'ld', ld_wo[1])
        DMA(ctx, 'y', DT['tap_wo0'], T['woslab0'][:, :], sem='out')
        DMA(ctx, 'y', DT['tap_wo1'], T['woslab1'][:, :], sem='out')
    ve_evac3 = [0] * KD
    for dc in range(KD):
        ws = dc // 2
        wbuf = T['woslab'][ws % 2]
        ctx.wait('t', 'ld', ctx.n('ld'))
        ctx.wait('t', 'sy', sy_abuf)
        bank = (dc % 2) * 2
        if dc >= 2:
            ctx.wait('t', 've', ve_evac3[dc - 2])
        for k in range(KD):
            for hf in range(2):
                MM(ctx, pb(bank, 2)[:, 512 * hf:512 * hf + 512],
                   wbuf[:, 256 * k + 128 * (dc % 2):256 * k + 128 * (dc % 2) + 128],
                   T['abuf'][:, 1024 * k + 512 * hf:1024 * k + 512 * hf + 512],
                   start=(k == 0), stop=(k == KD - 1))
        pe_mm = ctx.pemark()
        if dc % 2 == 1 and ws + 2 < 8:
            ctx.wait('g', 'pe', pe_mm)
            ld_wo[ws + 2] = DMA(ctx, 'g', T['woslab'][ws % 2][:, :].rearrange(
                "p (k e) -> p k e", k=KD),
                DT['woutT'].rearrange("(k p) e -> p k e", p=128)
                [:, :, 256 * (ws + 2):256 * (ws + 3)])
        ctx.wait('v', 'pe', pe_mm)
        ctx.wait('v', 'sy', sy_xc[dc])
        OP(ctx, 'v', 'tensor_scalar', T['tscr'][:, :], pb(bank, 2),
           T['ga'][:, dc:dc + 1], T['gabout'][:, dc:dc + 1], ALU.mult, ALU.add)
        ctx.wait('v', 've', ctx.n('ve'))
        ve_evac3[dc] = OP(ctx, 'v', 'tensor_add',
                          T['xnew'][:, 1024 * dc:1024 * (dc + 1)],
                          T['tscr'][:, :], T['xchunk'][dc % 2][:, :], sem='ve')
        if dc + 2 < KD:
            ctx.wait('y', 've', ve_evac3[dc])
            sy_xc[dc + 2] = DMA(ctx, 'y', T['xchunk'][dc % 2][:, :],
                                DT['x'][128 * (dc + 2):128 * (dc + 3), :],
                                sem='sy')
    ve_xnew = ctx.n('ve')
    if 'xnew' in _TAPS:
        ctx.wait('y', 've', ve_xnew)
        DMA(ctx, 'y', DT['tap_xnew'].rearrange("(k p) t -> p k t", p=128),
            T['xnew'][:, :].rearrange("p (k t) -> p k t", k=KD), sem='out')
    BAR(ctx)

    ctx.wait('y', 'out', ctx.n('out'))
    if _PHLIM <= 3:
        return
    pe_statk = [0] * KD
    for k in range(KD):
        if k >= 2:
            ctx.wait('v', 'pe', pe_statk[k - 2])
            ctx.wait('s', 'pe', pe_statk[k - 2])
        ve_xb = OP(ctx, 'v', 'tensor_copy', T['xbk'][k % 2][:, :],
                   T['xnew'][:, 1024 * k:1024 * (k + 1)], sem='ve')
        sc_sq = OP(ctx, 's', 'activation', T['sqk'][k % 2][:, :],
                   T['xnew'][:, 1024 * k:1024 * (k + 1)], AF.Square, sem='sc')
        ctx.wait('t', 've', ve_xb)
        ctx.wait('t', 'sc', sc_sq)
        for hf in range(2):
            MM(ctx, pb(0, 2)[0:1, 512 * hf:512 * hf + 512], T['invcol_b'][:, :],
               T['xbk'][k % 2][:, 512 * hf:512 * hf + 512],
               start=(k == 0), stop=(k == KD - 1))
            MM(ctx, pb(2, 2)[0:1, 512 * hf:512 * hf + 512], T['invcol_b'][:, :],
               T['sqk'][k % 2][:, 512 * hf:512 * hf + 512],
               start=(k == 0), stop=(k == KD - 1))
        pe_statk[k] = ctx.pemark()
    _ln_finalize()
    ctx.wait('v', 've', ctx.n('ve'))
    for k in range(KD):
        OP(ctx, 'v', 'tensor_sub', T['tscr'][:, :],
           T['xnew'][:, 1024 * k:1024 * (k + 1)], T['mb'][:, :])
        OP(ctx, 'v', 'tensor_mul', T['tscr'][:, :], T['tscr'][:, :], T['rb'][:, :])
        ctx.wait('v', 've', ctx.n('ve'))
        OP(ctx, 'v', 'tensor_scalar', T['hh'][:, 1024 * k:1024 * (k + 1)],
           T['tscr'][:, :], T['modp1'][:, 64 + k:65 + k],
           T['modr'][:, 48 + k:49 + k], ALU.mult, ALU.add, sem='ve')
    ve_h2 = ctx.n('ve')
    ctx.wait('y', 've', ve_h2)
    for k in range(KD):
        DMA(ctx, 'y', DT['x2_dram'][128 * k:128 * (k + 1), :],
            T['xnew'][:, 1024 * k:1024 * (k + 1)], sem='sy')
    if 'h2' in _TAPS:
        DMA(ctx, 'y', DT['tap_h2'].rearrange("(k p) t -> p k t", p=128),
            T['hh'][:, :].rearrange("p (k t) -> p k t", k=KD), sem='out')
    BAR(ctx)

    ctx.wait('y', 'out', ctx.n('out'))
    if _PHLIM <= 4:
        return
    ld_sel = DMA(ctx, 'g', T['sel8'][:, :], DT['sel8'])
    ld_a13 = DMA(ctx, 'g', T['a13s'][:, :], DT['a13s'])
    ve_wfull = 0
    for tt in range(8):
        ctx.wait('t', 've', ve_h2)
        if tt > 0:
            ctx.wait('t', 've', ve_wfull)
        for k in range(KD):
            MM(ctx, pb(4)[0:128, 0:8],
               T['hh'][:, 1024 * k + 128 * tt:1024 * k + 128 * tt + 128],
               T['gts'][:, 8 * k:8 * k + 8], start=(k == 0), stop=(k == KD - 1))
        pe_lg = ctx.pemark()
        ctx.wait('v', 'pe', pe_lg)
        lg = T['lgt']
        OP(ctx, 'v', 'tensor_copy', lg[:, :], pb(4)[0:128, 0:8], sem='ve')
        ctx.wait('v', 've', ctx.n('ve'))
        OP(ctx, 'v', 'reduce_max', T['m1c'][:, :], lg[:, :], axis=AX, sem='ve')
        ctx.wait('v', 've', ctx.n('ve'))
        OP(ctx, 'v', 'tensor_scalar', T['mk1'][:, :], lg[:, :], T['m1c'][:, 0:1],
           None, ALU.is_ge, sem='ve')
        ctx.wait('v', 've', ctx.n('ve'))
        OP(ctx, 'v', 'scalar_tensor_tensor', T['lgm'][:, :], T['mk1'][:, :],
           -1e30, lg[:, :], ALU.mult, ALU.add, sem='ve')
        ctx.wait('v', 've', ctx.n('ve'))
        OP(ctx, 'v', 'reduce_max', T['m2c'][:, :], T['lgm'][:, :], axis=AX,
           sem='ve')
        ctx.wait('v', 've', ctx.n('ve'))
        OP(ctx, 'v', 'tensor_sub', T['dmc'][:, :], T['m1c'][:, :], T['m2c'][:, :],
           sem='ve')
        OP(ctx, 'v', 'tensor_scalar', T['mk2'][:, :], lg[:, :], T['m2c'][:, 0:1],
           None, ALU.is_ge, sem='ve')
        ctx.wait('v', 've', ctx.n('ve'))
        ve_d = OP(ctx, 'v', 'tensor_sub', T['mk2'][:, :], T['mk2'][:, :],
                  T['mk1'][:, :], sem='ve')
        ctx.wait('s', 've', ve_d)
        sc_p1 = OP(ctx, 's', 'activation', T['p1c'][:, :], T['dmc'][:, :],
                   AF.Sigmoid, sem='sc')
        ctx.wait('v', 'sc', sc_p1)
        OP(ctx, 'v', 'tensor_scalar', T['p2c'][:, :], T['p1c'][:, 0:1], -1.0, 1.0,
           ALU.mult, ALU.add, sem='ve')
        OP(ctx, 'v', 'tensor_scalar', T['mk1'][:, :], T['mk1'][:, :],
           T['p1c'][:, 0:1], None, ALU.mult, sem='ve')
        ctx.wait('v', 've', ctx.n('ve'))
        OP(ctx, 'v', 'tensor_scalar', T['mk2'][:, :], T['mk2'][:, :],
           T['p2c'][:, 0:1], None, ALU.mult, sem='ve')
        ctx.wait('v', 've', ctx.n('ve'))
        ve_wfull = OP(ctx, 'v', 'tensor_add', T['wfull'][:, 8 * tt:8 * tt + 8],
                      T['mk1'][:, :], T['mk2'][:, :], sem='ve')
    if 'wfull' in _TAPS:
        ctx.wait('y', 've', ve_wfull)
        DMA(ctx, 'y', DT['tap_wfull'], T['wfull'][:, :], sem='out')
    ctx.wait('t', 've', ve_wfull)
    for tt in range(8):
        OP(ctx, 't', 'transpose', pb(0, 2)[0:8, 128 * tt:128 * tt + 128],
           T['wfull'][:, 8 * tt:8 * tt + 8], T['identf'][:, :])
    pe_wt = ctx.pemark()
    ctx.wait('v', 'pe', pe_wt)
    ve_wt = OP(ctx, 'v', 'tensor_copy', T['weT'][:, :], pb(0, 2)[0:8, :], sem='ve')
    ctx.wait('t', 've', ve_wt)
    ctx.wait('t', 'ld', ld_sel)
    ve_wb = 0
    for e in range(NE):
        for hf in range(2):
            if e + hf > 0:
                ctx.wait('t', 've', ve_wb)
            MM(ctx, pb(2), T['sel8'][:, 128 * e:128 * e + 128],
               T['weT'][0:8, 512 * hf:512 * hf + 512], start=True, stop=True)
            pe_wb = ctx.pemark()
            ctx.wait('v', 'pe', pe_wb)
            ve_wb = OP(ctx, 'v', 'tensor_copy',
                       T['weB'][:, 1024 * e + 512 * hf:1024 * e + 512 * hf + 512],
                       pb(2), sem='ve')
    ve_web = ve_wb
    ctx.wait('t', 'ld', ld_a13)
    sc_u = 0
    for ui in range(4):
        ctx.wait('t', 'sc', sc_u)
        for k in range(KD):
            for hf in range(2):
                MM(ctx, pb(4, 2)[:, 512 * hf:512 * hf + 512],
                   T['a13s'][:, 2048 * ui + 128 * k:2048 * ui + 128 * k + 128],
                   T['hh'][:, 1024 * k + 512 * hf:1024 * k + 512 * hf + 512],
                   start=(k == 0), stop=(k == KD - 1))
        pe_u = ctx.pemark()
        ctx.wait('s', 'pe', pe_u)
        sc_u = OP(ctx, 's', 'activation', T['uu'][ui][:, :], pb(4, 2), AF.Copy,
                  scale=LSC, sem='sc')
    sc_uall = sc_u

    ld_grp = [0] * 24
    pe_grp_last = [0] * 22

    def _issue_group(gi):
        buf = gi % 2
        DMA(ctx, 'g', T['w1grp'][buf][:, :].rearrange("p (k f) -> p k f", k=KD),
            DT['w1T'].rearrange("(k p) f -> p k f", p=128)
            [:, :, 256 * gi:256 * (gi + 1)])
        DMA(ctx, 'g', T['w3grp'][buf][:, :].rearrange("p (k f) -> p k f", k=KD),
            DT['w3T'].rearrange("(k p) f -> p k f", p=128)
            [:, :, 256 * gi:256 * (gi + 1)])
        DMA(ctx, 'g', T['b1grp'][buf][:, :].rearrange("p (cb f) -> p cb f", cb=2),
            DT['b1pad'].rearrange("p (cb f) -> p cb f", cb=2)
            [:, :, 256 * gi:256 * (gi + 1)])
        DMA(ctx, 'g', T['b3grp'][buf][:, :].rearrange("p (cb f) -> p cb f", cb=2),
            DT['b3pad'].rearrange("p (cb f) -> p cb f", cb=2)
            [:, :, 256 * gi:256 * (gi + 1)])
        return DMA(ctx, 'g', T['a2grp'][buf][:, :],
                   DT['a2s'][:, 256 * gi:256 * (gi + 1)])

    ctx.wait('g', 'sc', sc_uall)
    ld_grp[0] = _issue_group(0)
    ld_grp[1] = _issue_group(1)
    ctx.wait('v', 'pe', ctx.n('pe'))
    ve_u2z = OP(ctx, 'v', 'memset', pb(0, 4), 0.0, sem='ve')
    ctx.wait('t', 've', ve_u2z)
    sc_silu = {}
    ve_mul = {}
    pe_a2 = {}
    for c in range(FC):
        gi, ci = c // 2, c % 2
        buf = gi % 2
        ctx.wait('t', 'ld', ld_grp[gi])
        ctx.wait('t', 've', ve_web)
        ctx.wait('t', 'sc', sc_uall)
        if c > 0:
            ctx.wait('t', 'sc', sc_silu[(c - 1, NE - 1)])
        for k in range(KD):
            for hf in range(2):
                MM(ctx, pb(4, 2)[:, 512 * hf:512 * hf + 512],
                   T['w1grp'][buf][:, 256 * k + 128 * ci:256 * k + 128 * ci + 128],
                   T['hh'][:, 1024 * k + 512 * hf:1024 * k + 512 * hf + 512],
                   start=(k == 0), stop=(k == KD - 1))
        pe_b1 = ctx.pemark()
        if c > 0:
            ctx.wait('t', 've', ve_mul[(c - 1, NE - 1)])
        for k in range(KD):
            for hf in range(2):
                MM(ctx, pb(6, 2)[:, 512 * hf:512 * hf + 512],
                   T['w3grp'][buf][:, 256 * k + 128 * ci:256 * k + 128 * ci + 128],
                   T['hh'][:, 1024 * k + 512 * hf:1024 * k + 512 * hf + 512],
                   start=(k == 0), stop=(k == KD - 1))
        pe_b3 = ctx.pemark()
        ctx.wait('v', 'pe', pe_b1)
        ve_eb1 = OP(ctx, 'v', 'tensor_copy', T['base1'][:, :], pb(4, 2), sem='ve')
        ctx.wait('v', 'pe', pe_b3)
        ve_eb3 = OP(ctx, 'v', 'tensor_copy', T['base3'][:, :], pb(6, 2), sem='ve')
        for e in range(NE):
            jg, cb = e % 4, e // 4
            uo = 32 * jg
            if e == 0:
                ctx.wait('t', 've', ve_eb1)
            else:
                ctx.wait('t', 'sc', sc_silu[(c, e - 1)])
            for hf in range(2):
                MM(ctx, pb(4, 2)[:, 512 * hf:512 * hf + 512], T['identb'][:, :],
                   T['base1'][:, 512 * hf:512 * hf + 512], start=True, stop=False)
                MM(ctx, pb(4, 2)[:, 512 * hf:512 * hf + 512],
                   T['b1grp'][buf][uo:uo + 16,
                                   256 * cb + 128 * ci:256 * cb + 128 * ci + 128],
                   T['uu'][cb][uo:uo + 16, 512 * hf:512 * hf + 512],
                   start=False, stop=True, tp=(uo, 0))
            pe_h1 = ctx.pemark()
            ctx.wait('s', 'pe', pe_h1)
            if (c, e - 2) in ve_mul:
                ctx.wait('s', 've', ve_mul[(c, e - 2)])
            elif c > 0 and e < 2:
                ctx.wait('s', 've', ve_mul[(c - 1, NE - 2 + e)])
            sc_silu[(c, e)] = OP(ctx, 's', 'activation', T['sff'][e % 2][:, :],
                                 pb(4, 2), AF.Silu, sem='sc')
            if e == 0:
                ctx.wait('t', 've', ve_eb3)
            else:
                ctx.wait('t', 've', ve_mul[(c, e - 1)])
            for hf in range(2):
                MM(ctx, pb(6, 2)[:, 512 * hf:512 * hf + 512], T['identb'][:, :],
                   T['base3'][:, 512 * hf:512 * hf + 512], start=True, stop=False)
                MM(ctx, pb(6, 2)[:, 512 * hf:512 * hf + 512],
                   T['b3grp'][buf][uo:uo + 16,
                                   256 * cb + 128 * ci:256 * cb + 128 * ci + 128],
                   T['uu'][2 + cb][uo:uo + 16, 512 * hf:512 * hf + 512],
                   start=False, stop=True, tp=(uo, 0))
            pe_h3 = ctx.pemark()
            ctx.wait('v', 'pe', pe_h3)
            ctx.wait('v', 'sc', sc_silu[(c, e)])
            if (c, e - 2) in pe_a2:
                ctx.wait('v', 'pe', pe_a2[(c, e - 2)])
            elif c > 0 and e < 2:
                ctx.wait('v', 'pe', pe_a2[(c - 1, NE - 2 + e)])
            OP(ctx, 'v', 'tensor_mul', T['t2ff'][e % 2][:, :], T['sff'][e % 2][:, :],
               pb(6, 2), sem='ve')
            ctx.wait('v', 've', ctx.n('ve'))
            OP(ctx, 'v', 'tensor_mul', T['t2ff'][e % 2][:, :], T['t2ff'][e % 2][:, :],
               T['weB'][:, 1024 * e:1024 * (e + 1)], sem='ve')
            ctx.wait('v', 've', ctx.n('ve'))
            if e == 0:
                OP(ctx, 'v', 'tensor_copy', T['wact'][:, 1024 * c:1024 * (c + 1)],
                   T['t2ff'][e % 2][:, :], sem='ve')
            else:
                OP(ctx, 'v', 'tensor_add', T['wact'][:, 1024 * c:1024 * (c + 1)],
                   T['wact'][:, 1024 * c:1024 * (c + 1)], T['t2ff'][e % 2][:, :],
                   sem='ve')
            ve_mul[(c, e)] = ctx.n('ve')
            ctx.wait('t', 've', ve_mul[(c, e)])
            for hf in range(2):
                MM(ctx, pb(2 * cb + hf)[32 * jg:32 * jg + 16, :],
                   T['a2grp'][buf][:, 128 * ci + 16 * e:128 * ci + 16 * e + 16],
                   T['t2ff'][e % 2][:, 512 * hf:512 * hf + 512],
                   start=False, stop=(c == FC - 1),
                   tp=(0, 32 * jg))
            pe_a2[(c, e)] = ctx.pemark()
        pe_grp_last[gi] = pe_a2[(c, NE - 1)]
        if ci == 1 and gi + 2 < 22:
            ctx.wait('g', 'pe', pe_grp_last[gi])
            ld_grp[gi + 2] = _issue_group(gi + 2)
    pe_wact = ctx.n('pe')
    ve_wact = ctx.n('ve')
    if 'wact' in _TAPS:
        ctx.wait('y', 've', ve_wact)
        DMA(ctx, 'y', DT['tap_wact'].rearrange("(c p) t -> p c t", p=128),
            T['wact'][:, :].rearrange("p (c t) -> p c t", c=FC), sem='out')
    ctx.wait('s', 'pe', pe_wact)
    sc_u2 = OP(ctx, 's', 'activation', T['u2sb'][:, :], pb(0, 4), AF.Copy,
               scale=LSC, sem='sc')
    BAR(ctx)

    ctx.wait('y', 'out', ctx.n('out'))
    if _PHLIM <= 5:
        return
    ld_b2 = DMA(ctx, 'g', T['b2pad'][:, :], DT['b2pad'])
    ld_w2 = [0] * KD
    for dd in range(2):
        ld_w2[dd] = DMA(ctx, 'g',
                        T['w2slab'][dd][:, :].rearrange("p (f e) -> p f e", f=FC),
                        DT['w2T'].rearrange("(f p) e -> p f e", p=128)
                        [:, :, 128 * dd:128 * (dd + 1)])
    sy_xc2 = [0] * KD
    sy_xc2[0] = DMA(ctx, 'y', T['xchunk'][0][:, :], DT['x2_dram'][0:128, :],
                    sem='sy')
    sy_xc2[1] = DMA(ctx, 'y', T['xchunk'][1][:, :], DT['x2_dram'][128:256, :],
                    sem='sy')
    ve_out = [0] * KD
    out_dma = [0] * KD
    for dc in range(min(KD, _P6LIM)):
        ctx.wait('t', 'ld', ld_w2[dc])
        ctx.wait('t', 'ld', ld_b2)
        ctx.wait('t', 'sc', sc_u2)
        bank = 4 + (dc % 2) * 2
        if dc >= 2:
            ctx.wait('t', 've', ve_out[dc - 2])
        for f in range(FC):
            for hf in range(2):
                MM(ctx, pb(bank, 2)[:, 512 * hf:512 * hf + 512],
                   T['w2slab'][dc % 2][:, 128 * f:128 * (f + 1)],
                   T['wact'][:, 1024 * f + 512 * hf:1024 * f + 512 * hf + 512],
                   start=(f == 0), stop=(f == FC - 1 and hf == 1))
            if 4 <= f < 20:
                e = f - 4
                eidx, hf2 = e // 2, e % 2
                jg, cb = eidx % 4, eidx // 4
                MM(ctx, pb(bank, 2)[:, 512 * hf2:512 * hf2 + 512],
                   T['b2pad'][32 * jg:32 * jg + 16,
                              2048 * cb + 128 * dc:2048 * cb + 128 * dc + 128],
                   T['u2sb'][32 * jg:32 * jg + 16,
                             1024 * cb + 512 * hf2:1024 * cb + 512 * hf2 + 512],
                   start=False, stop=False, tp=(32 * jg, 0))
        pe_o2 = ctx.pemark()
        if dc + 2 < KD:
            ctx.wait('g', 'pe', pe_o2)
            ld_w2[dc + 2] = DMA(
                ctx, 'g', T['w2slab'][dc % 2][:, :].rearrange(
                    "p (f e) -> p f e", f=FC),
                DT['w2T'].rearrange("(f p) e -> p f e", p=128)
                [:, :, 128 * (dc + 2):128 * (dc + 3)])
        ctx.wait('v', 'pe', pe_o2)
        ctx.wait('v', 'sy', sy_xc2[dc])
        if dc >= 2:
            ctx.wait('v', 'out', out_dma[dc - 2])
        OP(ctx, 'v', 'tensor_scalar', T['tscr'][:, :], pb(bank, 2),
           T['gb'][:, dc:dc + 1], None, ALU.mult)
        ctx.wait('v', 've', ctx.n('ve'))
        ve_out[dc] = OP(ctx, 'v', 'tensor_add', T['ostage'][dc % 2][:, :],
                        T['tscr'][:, :], T['xchunk'][dc % 2][:, :], sem='ve')
        ctx.wait('y', 've', ve_out[dc])
        out_dma[dc] = DMA(ctx, 'y', DT['out'][128 * dc:128 * (dc + 1), :],
                          T['ostage'][dc % 2][:, :], sem='out')
        if dc + 2 < KD:
            sy_xc2[dc + 2] = DMA(ctx, 'y', T['xchunk'][dc % 2][:, :],
                                 DT['x2_dram'][128 * (dc + 2):128 * (dc + 3), :],
                                 sem='sy')
    ctx.wait('y', 'out', ctx.n('out'))
    BAR(ctx)


def _build(taps):
    nc = bass.Bass("TRN2", target_bir_lowering=False)
    DT = {}
    DT['x'] = nc.dram_tensor("x", [D, TOK], F32, kind="ExternalInput").ap()
    DT['wmod_s'] = nc.dram_tensor("wmod_s", [256, 6 * D], F32,
                                  kind="ExternalInput").ap()
    DT['cmat'] = nc.dram_tensor("cmat", [2, 128], F32, kind="ExternalInput").ap()
    for nm, sh in (('wqkvT', [D, 3 * D]), ('woutT', [D, D]), ('w1T', [D, FF]),
                   ('w3T', [D, FF]), ('w2T', [FF, D]), ('gts', [128, 128]),
                   ('a13s', [128, 4 * 2048]), ('a2s', [128, FC * 128]),
                   ('b1pad', [128, 2 * FF]), ('b3pad', [128, 2 * FF]),
                   ('b2pad', [128, 2 * D]), ('bq128', [128, 48]),
                   ('bo128', [128, 16]), ('bvrow', [1, D])):
        DT[nm] = nc.dram_tensor(nm, sh, F32, kind="ExternalInput").ap()
    DT['consts'] = nc.inline_tensor(
        np.tile(np.array([[EXPB, 1e-5, 0.0, 1.0]], np.float32), (128, 1)),
        "constrow_i").ap()
    DT['identb'] = nc.inline_tensor(np.eye(128, dtype=BF), "identb_i").ap()
    DT['identf'] = nc.inline_tensor(np.eye(128, dtype=np.float32), "identf_i").ap()
    DT['onesrow_f'] = nc.inline_tensor(np.ones((1, 128), np.float32),
                                       "onesrowf_i").ap()
    DT['onesrow_b'] = nc.inline_tensor(np.ones((1, 128), BF), "onesrowb_i").ap()
    DT['onescol_b'] = nc.inline_tensor(np.ones((128, 1), BF), "onescolb_i").ap()
    DT['invcol_b'] = nc.inline_tensor(np.full((128, 1), 1.0 / 2048.0, BF),
                                      "invcol_i").ap()
    sel = np.zeros((8, 8 * 128), np.float32)
    for e in range(8):
        sel[e, 128 * e:128 * (e + 1)] = 1.0
    DT['sel8'] = nc.inline_tensor(sel, "sel8_i").ap()
    ii = np.arange(128)[:, None]
    jj = np.arange(512)[None, :]
    msk = np.zeros((128, 4 * 512), np.float32)
    for m in range(4):
        msk[:, 512 * m:512 * (m + 1)] = ((ii + 128 * m) <= jj)
    DT['masks'] = nc.inline_tensor(msk.astype(BF), "masks_i").ap()
    DT['zrow'] = nc.inline_tensor(np.zeros((1, 512), BF), "zrow_i").ap()
    DT['x_dram'] = nc.dram_tensor("x_dram", [D, TOK], F32, kind="Internal").ap()
    DT['x2_dram'] = nc.dram_tensor("x2_dram", [D, TOK], F32, kind="Internal").ap()
    DT['mod_in'] = nc.dram_tensor("mod_in", [96, 128], F32, kind="Internal").ap()
    DT['mod_out'] = nc.dram_tensor("mod_out", [96, 128], F32, kind="Internal",
                                   addr_space="Shared").ap()
    DT['bar_in'] = nc.dram_tensor("bar_in", [8, 128], F32, kind="Internal").ap()
    DT['bar_out'] = nc.dram_tensor("bar_out", [8, 128], F32,
                                   kind="Internal").ap()
    for nm, sh in (('ci_q', [D, TOK]), ('co_q', [D, TOK]), ('ci_k', [D, TOK]),
                   ('co_k', [D, TOK]), ('ci_v', [S, 256]), ('co_v', [S, 256]),
                   ('ci_a', [D, TOK]), ('co_a', [D, TOK])):
        DT[nm] = nc.dram_tensor(nm, sh, BF16, kind="Internal").ap()
    DT['out'] = nc.dram_tensor("out", [D, TOK], F32, kind="ExternalOutput").ap()
    tapshapes = {'mod': ([128, 96], F32), 'h': ([D, TOK], BF16),
                 'qsend': ([D, TOK], BF16), 'vsend': ([S, 256], BF16),
                 'attres': ([D, TOK], BF16), 'xnew': ([D, TOK], F32),
                 'h2': ([D, TOK], BF16), 'wfull': ([128, 64], F32),
                 'wact': ([FC * 128, TOK], BF16),
                 'qm': ([128, 2 * 8192], BF16), 'kres': ([128, 2 * 8192], BF16),
                 'vres': ([128, 64 * 256], BF16), 'xdram': ([D, TOK], F32),
                 'wo01': None}
    for tp in taps:
        if tp == 'wo01':
            for t2 in ('wo0', 'wo1'):
                DT['tap_' + t2] = nc.dram_tensor("tap_" + t2, [128, KD * 256],
                                                 BF16, kind="ExternalOutput").ap()
            continue
        if tp == 'qkvres':
            for t2 in ('qm', 'kres', 'vres'):
                sh, dt = tapshapes[t2]
                DT['tap_' + t2] = nc.dram_tensor("tap_" + t2, sh, dt,
                                                 kind="ExternalOutput").ap()
            continue
        sh, dt = tapshapes[tp]
        DT['tap_' + tp] = nc.dram_tensor("tap_" + tp, sh, dt,
                                         kind="ExternalOutput").ap()

    arena = nc.alloc_sbuf_tensor("arena", [128, ARENA], mybir.dt.uint8)
    base = nc.lookup_mloc(arena).addr
    T = {}

    def place(name, off, shape, dtype):
        assert off % 32 == 0, name
        nb = int(np.prod(shape[1:])) * (2 if dtype == BF16 else 4)
        assert off + nb <= ARENA, (name, off, nb)
        T[name] = nc.alloc_sbuf_tensor_at(name, shape, dtype, offset=base + off)

    def dbl(name, off, shape, dtype, stride):
        place(name + '0', off, shape, dtype)
        place(name + '1', off + stride, shape, dtype)
        T[name] = [T[name + '0'], T[name + '1']]

    place('xf', A_OFF, [128, KD * 1024], F32)
    place('Qm', A_OFF, [128, 2 * 8192], BF16)
    place('xnew', A_OFF, [128, KD * 1024], F32)
    place('wact', A_OFF, [128, FC * 1024], BF16)
    dbl('woslab', A_OFF + 65536, [128, KD * 256], BF16, 8192)
    place('hh', B_OFF, [128, 2 * 8192], BF16)
    dbl('w2slab', B_OFF, [128, FC * 128], BF16, 11264)
    place('b2pad', B_OFF + 22528, [128, 2 * D], BF16)
    place('qksend', C_OFF, [128, 16 * 1024], BF16)
    place('Kres', C_OFF, [128, 2 * 8192], BF16)
    place('abuf', C_OFF, [128, KD * 1024], BF16)
    place('weB', C_OFF, [128, NE * 1024], BF16)
    T['uu'] = []
    for ui in range(4):
        place('uu%d' % ui, C_OFF + 16384 + 2048 * ui, [128, 1024], BF16)
        T['uu'].append(T['uu%d' % ui])
    place('base1', C_OFF + 24576, [128, 1024], BF16)
    place('base3', C_OFF + 26624, [128, 1024], BF16)
    place('u2sb', C_OFF + 28672, [128, 2048], BF16)
    dbl('wmslab', D_OFF, [128, 2 * 1024], F32, 8192)
    dbl('xbk', D_OFF + 16384, [128, 1024], BF16, 2048)
    dbl('sqk', D_OFF + 20480, [128, 1024], BF16, 2048)
    place('mb', D_OFF + 24576, [128, 1024], F32)
    place('rb', D_OFF + 28672, [128, 1024], F32)
    place('mrow', D_OFF + 32768, [1, 1024], F32)
    place('tscr', D_OFF + 32768, [128, 1024], F32)
    dbl('wslab', D_OFF, [128, KD * 512], BF16, 16384)
    place('Vres', D_OFF, [128, 64 * 256], BF16)
    place('masks', D_OFF + 32768, [128, 4 * 512], BF16)
    dbl('xchunk', D_OFF, [128, 1024], F32, 4096)
    dbl('ostage', D_OFF + 8192, [128, 1024], F32, 4096)
    place('a13s', D_OFF, [128, 4 * 2048], BF16)
    place('sel8', D_OFF + 16384, [8, 1024], F32)
    place('weT', D_OFF + 20480, [8, 1024], F32)
    dbl('w1grp', D_OFF, [128, KD * 256], BF16, 8192)
    dbl('w3grp', D_OFF + 16384, [128, KD * 256], BF16, 8192)
    dbl('b1grp', D_OFF + 32768, [128, 512], BF16, 1024)
    dbl('b3grp', D_OFF + 34816, [128, 512], BF16, 1024)
    dbl('a2grp', D_OFF + 36864, [128, 256], BF16, 512)
    eo = E_OFF

    def eplace(name, shape, dtype, nbytes):
        nonlocal eo
        place(name, eo, shape, dtype)
        eo += nbytes

    eplace('consts', [128, 4], F32, 32)
    eplace('identb', [128, 128], BF16, 256)
    eplace('identf', [128, 128], F32, 512)
    eplace('onesrow_f', [1, 128], F32, 512)
    eplace('onesrow_b', [1, 128], BF16, 256)
    eplace('onescol_b', [128, 1], BF16, 32)
    eplace('invcol_b', [128, 1], BF16, 32)
    eplace('bq128', [128, 48], F32, 192)
    eplace('bo128', [128, 16], F32, 64)
    eplace('gts', [128, 128], BF16, 256)
    eplace('cmat', [2, 128], F32, 512)
    eplace('siluc_f', [128, 2], F32, 32)
    eplace('zrow', [1, 512], BF16, 1024)
    eplace('modr', [128, 96], F32, 384)
    eplace('modp1', [128, 96], F32, 384)
    eplace('modloc', [128, 96], F32, 384)
    eplace('ga', [128, 16], F32, 64)
    eplace('gb', [128, 16], F32, 64)
    eplace('gsn', [128, 16], F32, 64)
    eplace('gabout', [128, 16], F32, 64)
    eplace('rowscr', [1, 1024], F32, 4096)
    eplace('lgt', [128, 8], F32, 32)
    eplace('lgm', [128, 8], F32, 32)
    eplace('m1c', [128, 1], F32, 32)
    eplace('m2c', [128, 1], F32, 32)
    eplace('dmc', [128, 1], F32, 32)
    eplace('p1c', [128, 1], F32, 32)
    eplace('p2c', [128, 1], F32, 32)
    eplace('mk1', [128, 8], F32, 32)
    eplace('mk2', [128, 8], F32, 32)
    eplace('wfull', [128, 64], F32, 256)
    ev = (eo + 31) // 32 * 32
    place('bvrow', ev, [1, D], BF16)
    dbl('vstage', ev + 4096, [128, 512], BF16, 1024)
    place('modstage', ev + 6144, [1, 512], F32)
    place('modin96', ev + 8192, [96, 128], F32)
    dbl('probs', ev, [128, 512], BF16, 1024)
    place('recsb', ev + 2048, [128, 512], F32)
    place('lnl', ev + 4096, [1, 512], F32)
    place('recrow', ev + 6144, [1, 512], F32)
    dbl('sff', ev, [128, 1024], BF16, 2048)
    dbl('t2ff', ev + 4096, [128, 1024], BF16, 2048)
    assert ev + 8192 <= ARENA, (ev, ARENA)

    Pg = nc.alloc_psum_tensor("pall", [128, 4096], F32)
    Pap = Pg.ap()

    import contextlib
    es = contextlib.ExitStack()
    sems = {}
    for snm in ('ld', 'sy', 'pe', 've', 'sc', 'c1', 'c2', 'c3', 'c4', 'c5', 'out'):
        sems[snm] = es.enter_context(nc.semaphore(snm))
    blk = es.enter_context(nc.Block())
    engines = {'g': nc.gpsimd, 't': nc.tensor, 'v': nc.vector, 's': nc.scalar,
               'y': nc.sync}
    with es:
        for cur, attr in (('g', 'gpsimd'), ('t', 'tensor'), ('v', 'vector'),
                          ('s', 'scalar'), ('y', 'sync')):
            def run(eng, cur=cur):
                ctx = Ctx(nc, cur, engines, sems)
                _emit(ctx, T, Pap, DT)
            getattr(blk, attr)(run)
    return nc


def _host_prep(inputs):
    x = np.asarray(inputs['L0_attn_x']).reshape(S, D)
    x_fm = np.ascontiguousarray(x.T)
    w_qkv = np.asarray(inputs['L0_attn_w_qkv'])
    w_out = np.asarray(inputs['L0_attn_w_out'])
    b_qkv = np.asarray(inputs['L0_attn_b_qkv'])
    b_out = np.asarray(inputs['L0_attn_b_out'])
    gate_w = np.asarray(inputs['L0_ffn_gate_w'])
    w1 = np.asarray(inputs['L0_ffn_w1'])
    w3 = np.asarray(inputs['L0_ffn_w3'])
    w2 = np.asarray(inputs['L0_ffn_w2'])
    a1 = np.asarray(inputs['L0_ffn_a1'])
    b1 = np.asarray(inputs['L0_ffn_b1'])
    a3 = np.asarray(inputs['L0_ffn_a3'])
    b3 = np.asarray(inputs['L0_ffn_b3'])
    a2 = np.asarray(inputs['L0_ffn_a2'])
    b2 = np.asarray(inputs['L0_ffn_b2'])
    c = np.asarray(inputs['L0_c'])
    wmod = np.asarray(inputs['L0_wmod'])

    sh = {}
    sh['wqkvT'] = np.ascontiguousarray(w_qkv.T)
    sh['woutT'] = np.ascontiguousarray(w_out.T)
    sh['w1T'] = np.ascontiguousarray(w1.T)
    sh['w3T'] = np.ascontiguousarray(w3.T)
    sh['w2T'] = np.ascontiguousarray(w2.T)
    sh['gts'] = np.ascontiguousarray(
        gate_w.T.reshape(16, 128, 8).transpose(1, 0, 2).reshape(128, 128))
    a13 = np.zeros((128, 4 * 2048), np.float32)
    for ui, (amat, eh) in enumerate(((a1, 0), (a1, 1), (a3, 0), (a3, 1))):
        for el in range(4):
            e = 4 * eh + el
            blkm = amat[e].T.reshape(16, 128, 16).transpose(1, 0, 2)
            for k in range(16):
                a13[:, 2048 * ui + 128 * k + 32 * el:
                    2048 * ui + 128 * k + 32 * el + 16] = blkm[:, k, :]
    sh['a13s'] = a13
    sh['a2s'] = np.ascontiguousarray(
        a2.transpose(2, 0, 1).reshape(FF, 128)
        .reshape(FC, 128, 128).transpose(1, 0, 2).reshape(128, FC * 128))
    for nm, bmat in (('b1pad', b1), ('b3pad', b3)):
        pad = np.zeros((128, 2 * FF), np.float32)
        for e in range(NE):
            jg, cb = e % 4, e // 4
            pad[32 * jg:32 * jg + 16, FF * cb:FF * (cb + 1)] = bmat[e].T
        sh[nm] = pad
    b2p = np.zeros((128, 2 * D), np.float32)
    for e in range(NE):
        jg, cb = e % 4, e // 4
        b2p[32 * jg:32 * jg + 16, D * cb:D * (cb + 1)] = b2[e].T
    sh['b2pad'] = b2p
    sh['bq128'] = np.ascontiguousarray(b_qkv.reshape(48, 128).T)
    sh['bo128'] = np.ascontiguousarray(b_out.reshape(16, 128).T)
    sh['bvrow'] = np.ascontiguousarray(b_qkv[4096:6144][None, :])

    per = []
    for r in range(NCORE):
        dd = dict(sh)
        dd['x'] = np.ascontiguousarray(np.concatenate(
            [x_fm[:, 512 * r:512 * (r + 1)],
             x_fm[:, 512 * (15 - r):512 * (16 - r)]], axis=1))
        dd['wmod_s'] = np.ascontiguousarray(wmod[256 * r:256 * (r + 1), :])
        dd['cmat'] = np.ascontiguousarray(c.reshape(16, 128)[[2 * r, 2 * r + 1], :])
        per.append(dd)
    return per


def kernel(**inputs):
    per = _host_prep(inputs)
    nc = _build(sorted(_TAPS))
    res = run_bass_kernel_spmd(nc, per, core_ids=list(range(NCORE)))
    full = np.zeros((D, S), np.float32)
    for r in range(NCORE):
        o = np.asarray(res.results[r]['out'])
        full[:, 512 * r:512 * (r + 1)] = o[:, 0:512]
        full[:, 512 * (15 - r):512 * (16 - r)] = o[:, 512:1024]
    out = np.ascontiguousarray(full.T).reshape(1, S, D).astype(np.float32)
    if _TAPS:
        kernel._tap_results = res.results
    return out
